# version 06
# speedup vs baseline: 2.3801x; 1.0844x over previous
import numpy as np
import ml_dtypes
import concourse.bass as bass
import concourse.mybir as mybir
from concourse.bass_utils import run_bass_kernel_spmd

F32 = mybir.dt.float32
BF16 = mybir.dt.bfloat16
I32 = mybir.dt.int32
AF = mybir.ActivationFunctionType
OP = mybir.AluOpType
AX = mybir.AxisListType

D = 1024
T = 4096
TO = 2048
NH = 16
HD = 64
NE = 8

BUILD_PHASES = 9
DEBUG = False

SEMS = ["ld", "ldg", "ldx0", "ldx1", "ldk0", "ldk1", "ldw0", "ldw1",
        "sp", "sp0", "sp1", "sp2", "sp3", "st", "te", "ac", "ve", "gp"]


class Ctx:
    def __init__(self, nc):
        self.nc = nc
        self.ap = {}
        self.t = {}
        self.sems = {}
        self.counts = {}
        self.marks = {}
        self.mode = "count"
        self.eng = None
        self.tag = ""
        self.uid = 0
        self._stack = None

    def sem(self, name):
        return self.sems[name]

    def do(self, engine, fn, inc=None, inc_n=1):
        if self.mode == "count":
            if inc is not None:
                self.counts[inc] += inc_n
            return
        if self.mode == engine:
            ins = fn(self.eng)
            if inc is not None:
                ins.then_inc(self.sems[inc], inc_n)

    def mark(self, name, sem):
        if self.mode == "count":
            self.marks[(self.tag, name)] = self.counts[sem]

    def wait(self, engine, sem, name):
        if self.mode == engine:
            v = self.marks[(self.tag, name)]
            if v > 0:
                self.eng.wait_ge(self.sems[sem], v)

    def dr(self, engine):
        if self.mode == engine:
            self.eng.drain()

    def waitx(self, engine, sem, tag, name):
        if self.mode == engine:
            v = self.marks[(tag, name)]
            if v > 0:
                self.eng.wait_ge(self.sems[sem], v)


_UNIQ = [0]


def uname(name):
    _UNIQ[0] += 1
    return f"{name}_u{_UNIQ[0]}"


def run_phase(K, phase_fn, *args, tag=None):
    K.tag = tag or phase_fn.__name__
    K.mode = "count"
    K.eng = None
    K.uid = 0
    phase_fn(K, *args)
    with K.nc.Block() as blk:
        def mk(engname):
            def body(eng):
                K.mode = engname
                K.eng = eng
                K.uid = 0
                phase_fn(K, *args)
            return body
        blk.sync(mk("sync"))
        blk.gpsimd(mk("gpsimd"))
        blk.tensor(mk("tensor"))
        blk.scalar(mk("scalar"))
        blk.vector(mk("vector"))
    K.mode = "count"



def ln_tile(K, dst, src, rms=False, weight_rep=None, eps_col="epsc",
            extra_add=None):
    t = K.t
    v, s = "vector", "scalar"
    uid = K.uid
    K.uid += 1
    sq, st_ = t["tmpf"], t["stats"]
    K.do(v, lambda e: e.tensor_tensor(sq[:, :], src, src, op=OP.mult))
    K.do(v, lambda e: e.reduce_sum(st_[:, 1:2], sq[:, :], axis=AX.X))
    K.dr(v)
    if rms:
        K.do(v, lambda e: e.tensor_scalar_mul(st_[:, 2:3], st_[:, 1:2], 1.0 / D), inc="ve")
        K.mark(("lnst", uid), "ve")
        K.wait(s, "ve", ("lnst", uid))
        K.do(s, lambda e: e.activation(st_[:, 4:5], st_[:, 2:3], AF.Sqrt,
                                       bias=t[eps_col][:, 0:1], scale=1.0), inc="ac")
        K.mark(("lnsq", uid), "ac")
        K.wait(v, "ac", ("lnsq", uid))
        K.do(v, lambda e: e.reciprocal(st_[:, 5:6], st_[:, 4:5]))
        K.dr(v)
        K.do(v, lambda e: e.scalar_tensor_tensor(
            dst, src, st_[:, 5:6], weight_rep, op0=OP.mult, op1=OP.mult), inc="ve")
    else:
        K.do(v, lambda e: e.reduce_sum(st_[:, 0:1], src, axis=AX.X))
        K.dr(v)
        K.do(v, lambda e: e.tensor_scalar_mul(st_[:, 2:3], st_[:, 0:1], 1.0 / D))
        K.do(v, lambda e: e.tensor_scalar_mul(st_[:, 3:4], st_[:, 1:2], 1.0 / D))
        K.dr(v)
        K.do(v, lambda e: e.scalar_tensor_tensor(
            st_[:, 6:7], st_[:, 2:3], st_[:, 2:3], st_[:, 3:4],
            op0=OP.mult, op1=OP.subtract), inc="ve")
        K.mark(("lnst", uid), "ve")
        K.wait(s, "ve", ("lnst", uid))
        K.do(s, lambda e: e.activation(st_[:, 4:5], st_[:, 6:7], AF.Sqrt,
                                       bias=t[eps_col][:, 0:1], scale=-1.0), inc="ac")
        K.mark(("lnsq", uid), "ac")
        K.wait(v, "ac", ("lnsq", uid))
        K.do(v, lambda e: e.reciprocal(st_[:, 5:6], st_[:, 4:5]))
        K.dr(v)
        if extra_add is None:
            K.do(v, lambda e: e.tensor_scalar(
                dst, src, st_[:, 2:3], st_[:, 5:6], op0=OP.subtract, op1=OP.mult),
                inc="ve")
        else:
            K.do(v, lambda e: e.tensor_scalar(
                sq[:, :], src, st_[:, 2:3], st_[:, 5:6], op0=OP.subtract, op1=OP.mult))
            K.do(v, lambda e: e.tensor_add(dst, extra_add, sq[:, :]), inc="ve")



def p0(K):
    t, ap = K.t, K.ap
    for name, src in [
        ("bq_c", "bq_c"), ("bkv_c", "bkv_c"), ("bo_c", "bo_c"),
        ("qkvb_c", "qkvb_c"), ("outb_c", "outb_c"),
        ("b1_c", "b1_c"), ("b2_c", "b2_c"),
        ("n1w", "n1w_rep"), ("n2w", "n2w_rep"),
    ]:
        K.do("sync", lambda e, n=name, s=src: e.dma_start(t[n][:, :], ap[s]),
             inc="ld", inc_n=16)
    for name, src in [("ident", "ident_bf"), ("ones_r64", "ones_r64"),
                      ("ones_r128", "ones_r128"), ("ones_hi128", "ones_hi128"),
                      ("ones_col", "ones_col")]:
        K.do("gpsimd", lambda e, n=name, s=src: e.dma_start(t[n][:, :], ap[s]),
             inc="ldg", inc_n=16)
    K.do("gpsimd", lambda e: e.dma_start(
        t["wq"][:, :].rearrange("p (kc o) -> p kc o", kc=8),
        ap["wq_t"].rearrange("(kc p) o -> p kc o", p=128)),
        inc="ldg", inc_n=16)
    K.do("gpsimd", lambda e: e.dma_start(
        t["wkv"][:, :].rearrange("p (kc o) -> p kc o", kc=8),
        ap["wkv_t"].rearrange("(kc p) o -> p kc o", p=128)),
        inc="ldg", inc_n=16)
    K.mark("ld", "ld")
    K.mark("ldg", "ldg")
    K.wait("vector", "ld", "ld")
    K.do("vector", lambda e: e.memset(t["epsc"][:, :], 1e-5))
    K.do("vector", lambda e: e.memset(t["eps6c"][:, :], 1e-6))
    K.do("vector", lambda e: e.tensor_scalar_mul(t["bq_s"][:, :], t["bq_c"][:, :], 0.125))
    K.do("vector", lambda e: e.tensor_scalar_mul(
        t["qkvb_qs"][:, :], t["qkvb_c"][:, 0:8], 0.125), inc="ve")
    K.mark("ve", "ve")



def uT_dest(K, nm, tt):
    return K.t[nm][:, :].rearrange("p (fc c) -> p fc c", fc=8)[:, :, tt * 128:(tt + 1) * 128]


def p1(K):
    t, ap = K.t, K.ap
    for bl in range(8):
        sl = bl % 2
        ldsem = f"ldx{sl}"
        own = bl < 4
        srcb = ap["x_own"] if own else ap["x_q"]
        boff = bl if own else bl - 4
        if bl >= 1:
            K.wait("sync", "ve", ("u_done", bl - 1))
        K.do("sync", lambda e, srcb=srcb, boff=boff: e.dma_start(
            t["xb0"][:, :].rearrange("p (tt d) -> p tt d", tt=4),
            srcb.rearrange("(bk tt p) d -> bk p tt d", tt=4, p=128)[boff]),
            inc=ldsem, inc_n=16)
        K.mark(("x", bl), ldsem)

        K.wait("vector", ldsem, ("x", bl))
        if bl >= 1:
            K.wait("vector", "te", ("tp_done", bl - 1))
        if bl == 0:
            K.waitx("vector", "ld", "p0", "ld")
        for tt in range(4):
            ln_tile(K, t["ub0"][:, tt * D:(tt + 1) * D],
                    t["xb0"][:, tt * D:(tt + 1) * D])
        K.mark(("u_done", bl), "ve")

        K.wait("tensor", "ve", ("u_done", bl))
        if bl == 0:
            K.waitx("tensor", "ldg", "p0", "ldg")
        for tt in range(4):
            idx = bl * 4 + tt
            pt = t[f"ptp{tt % 2}"]
            if idx >= 2:
                K.wait("tensor", "ac", ("tpev", idx - 2))
            for fc in range(8):
                K.do("tensor", lambda e, tt=tt, fc=fc, pt=pt: e.transpose(
                    pt[:, fc * 128:(fc + 1) * 128],
                    t["ub0"][:, tt * D + fc * 128: tt * D + (fc + 1) * 128],
                    t["ident"][:, :]), inc="te" if fc == 7 else None)
            K.mark(("tp", idx), "te")
            K.wait("scalar", "te", ("tp", idx))
            if bl >= 1 and tt == 0:
                K.wait("scalar", "te", ("proj", bl - 1))
            K.do("scalar", lambda e, tt=tt, pt=pt: e.copy(
                uT_dest(K, "uT0", tt), pt[:, :]), inc="ac")
            K.mark(("tpev", idx), "ac")
        K.mark(("tp_done", bl), "te")
        K.mark(("uT", bl), "ac")

        K.wait("tensor", "ac", ("uT", bl))
        if own:
            if bl >= 1:
                K.wait("tensor", "ac", ("kvev", bl - 1))
            for kc in range(8):
                K.do("tensor", lambda e, kc=kc: e.matmul(
                    t["pskv"][:, :], t["wkv"][:, kc * 128:(kc + 1) * 128],
                    t["uT0"][:, kc * 512:(kc + 1) * 512],
                    start=(kc == 0), stop=(kc == 7)), inc="te" if kc == 7 else None)
            K.mark(("kv", bl), "te")
            K.wait("scalar", "te", ("kv", bl))
            K.do("scalar", lambda e, bl=bl: e.activation(
                t["kvTh"][:, bl * 512:(bl + 1) * 512], t["pskv"][:, :], AF.Identity,
                bias=t["bkv_c"][:, 0:1], scale=1.0), inc="ac")
            K.mark(("kvev", bl), "ac")
        else:
            for oc in range(8):
                idx = (bl - 4) * 8 + oc
                psq = t[f"psq{oc % 2}"]
                if idx >= 2 or bl > 4:
                    K.wait("tensor", "ac", ("qev", idx - 2)) if idx >= 2 else None
                for kc in range(8):
                    K.do("tensor", lambda e, oc=oc, kc=kc, psq=psq: e.matmul(
                        psq[:, :],
                        t["wq"][:, kc * D + oc * 128: kc * D + (oc + 1) * 128],
                        t["uT0"][:, kc * 512:(kc + 1) * 512],
                        start=(kc == 0), stop=(kc == 7)), inc="te" if kc == 7 else None)
                K.mark(("q", idx), "te")
                K.wait("scalar", "te", ("q", idx))
                if bl == 4 and oc == 0:
                    K.waitx("scalar", "ve", "p0", "ve")
                K.do("scalar", lambda e, oc=oc, boff=boff, psq=psq: e.activation(
                    t["qTa"][:, oc * TO + boff * 512: oc * TO + boff * 512 + 512],
                    psq[:, :], AF.Identity, bias=t["bq_s"][:, oc:oc + 1],
                    scale=0.125), inc="ac")
                K.mark(("qev", idx), "ac")
        K.mark(("proj", bl), "te")
    K.wait("sync", "ac", ("kvev", 3))
    K.do("sync", lambda e: e.dma_start(ap["cc1_in"], t["kvTh"][:, :]),
         inc="sp", inc_n=16)
    K.mark("kvsp", "sp")
    K.wait("gpsimd", "sp", "kvsp")
    K.do("gpsimd", lambda e: e.collective_compute(
        "AllGather", OP.bypass, replica_groups=[[0, 1], [2, 3], [4, 5], [6, 7]],
        ins=[K.h["cc1_in"].ap().opt()], outs=[K.h["cc1_out"].ap().opt()]),
        inc="gp")
    K.mark("cc1", "gp")
    K.mark("te_end", "te")
    K.mark("ac_end", "ac")



def p2(K):
    t, ap = K.t, K.ap
    for j in range(8):
        K.do("vector", lambda e, j=j: e.tensor_tensor(
            t["sqb"][:, 0:TO], t["qTa"][:, j * TO:(j + 1) * TO],
            t["qTa"][:, j * TO:(j + 1) * TO], op=OP.mult), inc="ve")
        K.mark(("qsq", j), "ve")
        K.wait("tensor", "ve", ("qsq", j))
        for ns in range(4):
            idx = j * 4 + ns
            pn = t[f"pn{ns % 2}"]
            if idx >= 2:
                K.wait("tensor", "ve", ("qnred", idx - 2))
            K.do("tensor", lambda e, ns=ns, pn=pn: e.matmul(
                pn[0:1, :], t["ones_col"][:, :], t["sqb"][:, ns * 512:(ns + 1) * 512],
                start=True, stop=True), inc="te")
            K.mark(("qnmm", idx), "te")
            K.wait("vector", "te", ("qnmm", idx))
            K.do("vector", lambda e, idx=idx, pn=pn: e.reduce_max(
                t["nrowq"][0:1, idx:idx + 1], pn[0:1, :], axis=AX.X), inc="ve")
            K.mark(("qnred", idx), "ve")
    K.waitx("gpsimd", "gp", "p1", "cc1")
    K.do("gpsimd", lambda e: e.dma_start(
        t["kvT"][:, :].rearrange("p (g c) -> p g c", g=2),
        ap["cc1_out"].rearrange("g p c -> p g c")), inc="ldg", inc_n=16)
    K.mark("kvl", "ldg")
    K.wait("gpsimd", "ldg", "kvl")
    K.do("gpsimd", lambda e: e.dma_start(t["kdup"][64:128, :], t["kvT"][0:64, :]),
         inc="ldg", inc_n=16)
    K.mark("kdup", "ldg")
    K.do("vector", lambda e: e.memset(t["v_lo"][:, :], 0.0))
    K.do("vector", lambda e: e.memset(
        t["v_lo"][:, :].rearrange("p (kc c) -> p kc c", c=128)[:, :, 64:65], 1.0))
    K.do("vector", lambda e: e.memset(t["v_hi"][:, :], 0.0))
    K.do("vector", lambda e: e.memset(
        t["v_hi"][:, :].rearrange("p (kc c) -> p kc c", c=128)[:, :, 0:1], 1.0),
        inc="ve")
    K.mark("vinit", "ve")
    K.wait("tensor", "ve", "vinit")
    K.wait("tensor", "ldg", "kvl")
    for g in range(4):
        if g >= 1:
            K.wait("tensor", "ac", ("vev", g - 1))
        for i in range(8):
            kc = g * 8 + i
            K.do("tensor", lambda e, kc=kc, i=i: e.transpose(
                t["ptpv"][:, i * 128:(i + 1) * 128],
                t["kvT"][:, kc * 128:(kc + 1) * 128], t["ident"][:, :]),
                inc="te" if i == 7 else None)
        K.mark(("vtp", g), "te")
        K.wait("scalar", "te", ("vtp", g))
        K.do("scalar", lambda e, g=g: e.copy(
            t["v_lo"][:, :].rearrange("p (kc c) -> p kc c", c=128)[:, g * 8:(g + 1) * 8, 0:64],
            t["ptpv"][:, :].rearrange("p (i c) -> p i c", c=128)[:, :, 64:128]))
        K.do("scalar", lambda e, g=g: e.copy(
            t["v_hi"][:, :].rearrange("p (kc c) -> p kc c", c=128)[:, g * 8:(g + 1) * 8, 64:128],
            t["ptpv"][:, :].rearrange("p (i c) -> p i c", c=128)[:, :, 64:128]),
            inc="ac")
        K.mark(("vev", g), "ac")
    K.wait("vector", "ldg", "kvl")
    K.do("vector", lambda e: e.tensor_tensor(
        t["sqb"][:, :], t["kvT"][:, :], t["kvT"][:, :], op=OP.mult), inc="ve")
    K.mark("ksq", "ve")
    K.wait("tensor", "ve", "ksq")
    for ns in range(8):
        pn = t[f"pn{ns % 2}"]
        if ns >= 2:
            K.wait("tensor", "ve", ("knred", ns - 2))
        K.do("tensor", lambda e, ns=ns, pn=pn: e.matmul(
            pn[0:1, :], t["ones_col"][:, :], t["sqb"][:, ns * 512:(ns + 1) * 512],
            start=True, stop=True), inc="te")
        K.mark(("knmm", ns), "te")
        K.wait("vector", "te", ("knmm", ns))
        K.do("vector", lambda e, ns=ns, pn=pn: e.reduce_max(
            t["nrowk"][0:1, ns:ns + 1], pn[0:1, :], axis=AX.X), inc="ve")
        K.mark(("knred", ns), "ve")
    K.dr("vector")
    K.do("vector", lambda e: e.reduce_max(t["nrm"][0:1, 0:1], t["nrowq"][0:1, 0:32], axis=AX.X))
    K.do("vector", lambda e: e.reduce_max(t["nrm"][0:1, 1:2], t["nrowk"][0:1, :], axis=AX.X))
    K.dr("vector")
    K.do("vector", lambda e: e.tensor_tensor(
        t["nrm"][0:1, 2:3], t["nrm"][0:1, 0:1], t["nrm"][0:1, 1:2], op=OP.mult),
        inc="ve")
    K.mark("nprod", "ve")
    K.wait("scalar", "ve", "nprod")
    K.do("scalar", lambda e: e.activation(
        t["nrm"][0:1, 3:4], t["nrm"][0:1, 2:3], AF.Sqrt,
        bias=t["eps6c"][0:1, 0:1], scale=1.0), inc="ac")
    K.mark("nsqrt", "ac")
    K.wait("vector", "ac", "nsqrt")
    K.do("vector", lambda e: e.tensor_scalar(
        t["nrm"][0:1, 4:5], t["nrm"][0:1, 3:4], -1.0, 30.0,
        op0=OP.mult, op1=OP.add))
    K.dr("vector")
    K.do("vector", lambda e: e.tensor_copy(t["nrmb"][0:1, 0:1], t["nrm"][0:1, 4:5]),
         inc="ve")
    K.mark("nbf", "ve")
    K.wait("tensor", "ve", "nbf")
    K.do("tensor", lambda e: e.matmul(
        t["pn0"][:, 0:1], t["ones_r128"][:, :], t["nrmb"][0:1, 0:1],
        start=True, stop=True), inc="te")
    K.mark("nbc", "te")
    K.wait("scalar", "te", "nbc")
    K.do("scalar", lambda e: e.copy(t["mbarA"][:, :], t["pn0"][:, 0:1]), inc="ac")
    K.mark("mbar", "ac")



def p3(K):
    t = K.t
    K.waitx("tensor", "ldg", "p2", "kdup")

    def yparts(u):
        j_, qc_ = divmod(u, 4)
        rows = slice(0, 64) if qc_ < 2 else slice(64, 128)
        ch0 = (qc_ * 4) % 8
        aA, aB = ("accA", "accB") if u % 2 == 0 else ("accA2", "accB2")
        return [(rows, ch0 * TO + 2 * j_, aA, "bcbA"),
                (rows, ch0 * TO + 2 * j_ + 1, aB, "bcbB")]

    def accn(u):
        return ("accA", "accB") if u % 2 == 0 else ("accA2", "accB2")

    def bc_pair(u):
        rows = slice(0, 64) if (u % 4) < 2 else slice(64, 128)
        aA, aB = accn(u)
        K.wait("tensor", "ve", ("accfree", u))
        for ps, cs in ((aA, 0), (aB, 512)):
            K.do("tensor", lambda e, ps=ps, cs=cs, rows=rows: e.matmul(
                t[ps][rows, :], t["ones_r64"][:, :],
                t["sums"][0:1, cs:cs + 512],
                start=True, stop=True), inc="te" if cs else None)
        K.mark(("bc", u), "te")

    def norm_consume(u):
        aA, aB = accn(u)
        K.wait("vector", "te", ("bc", u))

        def _lp(e, fn, *a, **kw):
            with K.nc.allow_low_precision(reason="softmax normalization in bf16"):
                return fn(*a, **kw)
        rows0 = slice(0, 64) if (u % 4) < 2 else slice(64, 128)
        K.do("vector", lambda e, rows0=rows0, aA=aA: _lp(
            e, e.tensor_copy, t["bcbA"][rows0, :], t[aA][rows0, :]))
        K.do("vector", lambda e, rows0=rows0, aB=aB: _lp(
            e, e.tensor_copy, t["bcbB"][rows0, :], t[aB][rows0, :]), inc="ve")
        K.mark(("bcc", u), "ve")
        K.dr("vector")
        K.do("vector", lambda e, rows0=rows0: _lp(
            e, e.reciprocal, t["bcbA"][rows0, :], t["bcbA"][rows0, :]))
        K.do("vector", lambda e, rows0=rows0: _lp(
            e, e.reciprocal, t["bcbB"][rows0, :], t["bcbB"][rows0, :]))
        K.dr("vector")
        for rows, base, acc, bcb in yparts(u):
            K.do("vector", lambda e, rows=rows, base=base, bcb=bcb: e.tensor_tensor(
                t["yTa"][rows, base: base + 16 * 511 + 1: 16],
                t["yTa"][rows, base: base + 16 * 511 + 1: 16],
                t[bcb][rows, 0:512], op=OP.mult),
                inc="ve" if bcb == "bcbB" else None)
        K.mark(("norm", u), "ve")

    for j in range(8):
        for qc in range(4):
            u = j * 4 + qc
            lo = qc < 2
            vsel = "v_lo" if lo else "v_hi"
            aA, aB = accn(u)
            for kc in range(32):
                sAB = t[f"sAB{kc % 2}"]
                if kc >= 2:
                    K.wait("tensor", "ac", ("exp", u, kc - 2))
                elif u > 0:
                    K.wait("tensor", "ac", ("exp", u - 1, 30 + kc))
                K.do("tensor", lambda e, j=j, qc=qc, kc=kc, sAB=sAB: e.matmul(
                    sAB[:, 0:512], t["kvT"][0:64, kc * 128:(kc + 1) * 128],
                    t["qTa"][0:64, j * TO + qc * 512: j * TO + qc * 512 + 512],
                    start=True, stop=True))
                K.do("tensor", lambda e, j=j, qc=qc, kc=kc, sAB=sAB: e.matmul(
                    sAB[:, 512:1024], t["kdup"][64:128, kc * 128:(kc + 1) * 128],
                    t["qTa"][64:128, j * TO + qc * 512: j * TO + qc * 512 + 512],
                    start=True, stop=True), inc="te")
                K.mark(("sc", u, kc), "te")
                if kc == 0 and u > 0:
                    vsp = "v_lo" if ((u - 1) % 4) < 2 else "v_hi"
                    aAp, aBp = accn(u - 1)
                    K.wait("tensor", "ac", ("exp", u - 1, 31))
                    K.do("tensor", lambda e, vsp=vsp, aAp=aAp: e.matmul(
                        t[aAp][:, :], t[vsp][:, 31 * 128:32 * 128],
                        t["PT1"][:, 0:512], start=False, stop=True))
                    K.do("tensor", lambda e, vsp=vsp, aBp=aBp: e.matmul(
                        t[aBp][:, :], t[vsp][:, 31 * 128:32 * 128],
                        t["PT1"][:, 512:1024], start=False, stop=True), inc="te")
                    K.mark(("pv", u - 1, 31), "te")
                if kc == 6 and u > 0:
                    bc_pair(u - 1)
                if kc >= 1:
                    kp = kc - 1
                    K.wait("tensor", "ac", ("exp", u, kp))
                    if kp == 0 and u >= 2:
                        K.wait("tensor", "ve", ("bcc", u - 2))
                    PT = t[f"PT{kp % 2}"]
                    K.do("tensor", lambda e, kp=kp, PT=PT, vsel=vsel, aA=aA: e.matmul(
                        t[aA][:, :], t[vsel][:, kp * 128:(kp + 1) * 128],
                        PT[:, 0:512], start=(kp == 0), stop=False))
                    K.do("tensor", lambda e, kp=kp, PT=PT, vsel=vsel, aB=aB: e.matmul(
                        t[aB][:, :], t[vsel][:, kp * 128:(kp + 1) * 128],
                        PT[:, 512:1024], start=(kp == 0), stop=False), inc="te")
                    K.mark(("pv", u, kp), "te")
            if u == 31:
                K.wait("tensor", "ac", ("exp", u, 31))
                K.do("tensor", lambda e, vsel=vsel, aA=aA: e.matmul(
                    t[aA][:, :], t[vsel][:, 31 * 128:32 * 128],
                    t["PT1"][:, 0:512], start=False, stop=True))
                K.do("tensor", lambda e, vsel=vsel, aB=aB: e.matmul(
                    t[aB][:, :], t[vsel][:, 31 * 128:32 * 128],
                    t["PT1"][:, 512:1024], start=False, stop=True), inc="te")
                K.mark(("pv", u, 31), "te")
            for kc in range(32):
                K.wait("scalar", "te", ("sc", u, kc))
                if u == 0 and kc == 0:
                    K.waitx("scalar", "ac", "p2", "mbar")
                K.do("scalar", lambda e, kc=kc: e.activation(
                    t[f"PT{kc % 2}"][:, :], t[f"sAB{kc % 2}"][:, :], AF.Exp,
                    bias=t["mbarA"][:, 0:1], scale=1.0), inc="ac")
                K.mark(("exp", u, kc), "ac")
            if u > 0:
                norm_consume(u - 1)
            K.wait("vector", "te", ("pv", u, 31))
            if u > 0:
                K.wait("vector", "te", ("bc", u - 1))
            srow = 64 if lo else 0

            def _cp(e, dst, src_):
                with K.nc.allow_low_precision(reason="unnormalized evac in bf16"):
                    return e.tensor_copy(dst, src_)
            K.do("vector", lambda e, srow=srow, aA=aA: _cp(
                e, t["sums"][0:1, 0:512], t[aA][srow:srow + 1, :]))
            K.do("vector", lambda e, srow=srow, aB=aB: _cp(
                e, t["sums"][0:1, 512:1024], t[aB][srow:srow + 1, :]))
            for rows, base, acc, bcb in yparts(u):
                K.do("vector", lambda e, rows=rows, base=base, acc=acc: _cp(
                    e, t["yTa"][rows, base: base + 16 * 511 + 1: 16],
                    t[acc][rows, 0:512]),
                    inc="ve" if acc.startswith("accB") else None)
            K.mark(("accfree", u), "ve")
    bc_pair(31)
    norm_consume(31)
    K.mark("ve_end", "ve")



def p4a(K):
    t, ap = K.t, K.ap
    K.do("gpsimd", lambda e: e.dma_start(
        t["wo"][:, :].rearrange("p (kc o) -> p kc o", kc=8),
        ap["wo_t"].rearrange("(kc p) o -> p kc o", p=128)), inc="ldg", inc_n=16)
    K.mark("wo", "ldg")
    K.wait("tensor", "ldg", "wo")
    for bl in range(4):
        for oc in range(8):
            idx = bl * 8 + oc
            psq = t[f"psA{oc % 2}"]
            if idx >= 2:
                K.wait("tensor", "ac", ("aev", idx - 2))
            for kc in range(8):
                K.do("tensor", lambda e, oc=oc, kc=kc, bl=bl, psq=psq: e.matmul(
                    psq[:, :],
                    t["wo"][:, kc * D + oc * 128: kc * D + (oc + 1) * 128],
                    t["yTa"][:, kc * TO + bl * 512: kc * TO + bl * 512 + 512],
                    start=(kc == 0), stop=(kc == 7)), inc="te" if kc == 7 else None)
            K.mark(("amm", idx), "te")
            K.wait("scalar", "te", ("amm", idx))
            K.do("scalar", lambda e, oc=oc, psq=psq: e.activation(
                t["aT"][:, oc * 512:(oc + 1) * 512], psq[:, :],
                AF.Identity, bias=t["bo_c"][:, oc:oc + 1], scale=1.0), inc="ac")
            K.mark(("aev", idx), "ac")
        for tt in range(4):
            it = bl * 4 + tt
            pt = t[f"ptp{tt % 2}"]
            xs = t[f"xt{tt % 2}"]
            x1s = t[f"x1t{tt % 2}"]
            spsem = f"sp{tt % 2}"
            if it >= 2:
                K.wait("sync", "ve", ("x1done", it - 2))
            K.do("sync", lambda e, it=it, xs=xs: e.dma_start(
                xs[:, :], ap["x_own"][it * 128:(it + 1) * 128, :]),
                inc=f"ldx{tt % 2}", inc_n=16)
            K.mark(("xt", it), f"ldx{tt % 2}")
            K.wait("tensor", "ac", ("aev", bl * 8 + 7))
            if it >= 2:
                K.wait("tensor", "ac", ("atok_ev", it - 2))
            for oc in range(8):
                K.do("tensor", lambda e, oc=oc, tt=tt, pt=pt: e.transpose(
                    pt[:, oc * 128:(oc + 1) * 128],
                    t["aT"][:, oc * 512 + tt * 128: oc * 512 + (tt + 1) * 128],
                    t["ident"][:, :]), inc="te" if oc == 7 else None)
            K.mark(("atp", it), "te")
            K.wait("scalar", "te", ("atp", it))
            if it >= 2:
                K.wait("scalar", "ve", ("atok_used", it - 2))
            K.do("scalar", lambda e, pt=pt, tt=tt: e.copy(
                t[f"atok{tt % 2}"][:, :], pt[:, :]), inc="ac")
            K.mark(("atok_ev", it), "ac")
            K.wait("vector", f"ldx{tt % 2}", ("xt", it))
            K.wait("vector", "ac", ("atok_ev", it))
            if it >= 2:
                K.wait("vector", spsem, ("x1sp", it - 2))
            if it >= 1:
                K.wait("vector", "te", ("rtp", it - 1))
            ln_tile(K, x1s[:, :], t[f"atok{tt % 2}"][:, :], extra_add=xs[:, :])
            K.mark(("atok_used", it), "ve")
            K.mark(("x1done", it), "ve")
            K.wait("sync", "ve", ("x1done", it))
            K.do("sync", lambda e, it=it, x1s=x1s: e.dma_start(
                ap["x1_sp"][it * 128:(it + 1) * 128, :], x1s[:, :]),
                inc=spsem, inc_n=16)
            K.mark(("x1sp", it), spsem)
            ln_tile(K, t["v0t"][:, :], x1s[:, :])
            ln_tile(K, t["r1tok"][:, :], t["v0t"][:, :], rms=True,
                    weight_rep=t["n1w"][:, :], eps_col="eps6c")
            K.mark(("r1done", it), "ve")
            K.wait("tensor", "ve", ("r1done", it))
            if it >= 1:
                K.wait("tensor", "ac", ("r1ev", it - 1))
            for fc in range(8):
                K.do("tensor", lambda e, fc=fc: e.transpose(
                    t["ptpr"][:, fc * 128:(fc + 1) * 128],
                    t["r1tok"][:, fc * 128:(fc + 1) * 128],
                    t["ident"][:, :]), inc="te" if fc == 7 else None)
            K.mark(("rtp", it), "te")
            K.wait("scalar", "te", ("rtp", it))
            if bl > 0 and tt == 0:
                K.wait("scalar", "sp", ("r1sp", bl - 1))
            K.do("scalar", lambda e, tt=tt: e.copy(
                uT_dest(K, "r1T", tt), t["ptpr"][:, :]), inc="ac")
            K.mark(("r1ev", it), "ac")
        K.wait("sync", "ac", ("r1ev", bl * 4 + 3))
        K.do("sync", lambda e, bl=bl: e.dma_start(
            ap["r1_sp"][:, :, bl * 512:(bl + 1) * 512],
            t["r1T"][:, :].rearrange("p (kc c) -> p kc c", kc=8)), inc="sp", inc_n=16)
        K.mark(("r1sp", bl), "sp")
    K.mark("sp_end", "sp")
    K.wait("sync", "sp", "sp_end")
    K.wait("sync", "sp0", ("x1sp", 14))
    K.wait("sync", "sp1", ("x1sp", 15))


def p4b(K):
    t, ap = K.t, K.ap
    K.do("gpsimd", lambda e: e.dma_start(
        t["qkvw"][:, :].rearrange("p (kc o) -> p kc o", kc=8),
        ap["qkvw_t"].rearrange("(kc p) o -> p kc o", p=128)), inc="ldg", inc_n=16)
    K.do("gpsimd", lambda e: e.dma_start(t["rotm"][:, :], ap["rotm"]), inc="ldg", inc_n=16)
    K.do("sync", lambda e: e.dma_start(t["cosb"][:, :], ap["cosO"]), inc="ld", inc_n=16)
    K.do("sync", lambda e: e.dma_start(t["sinb"][:, :], ap["sinO"]), inc="ld", inc_n=16)
    for bl in range(4):
        K.do("sync", lambda e, bl=bl: e.dma_start(
            t[f"r1Tb{bl}"][:, :].rearrange("p (kc c) -> p kc c", kc=8),
            ap["r1_sp"][:, :, bl * 512:(bl + 1) * 512]), inc="ldk0", inc_n=16)
    K.mark("r1l", "ldk0")
    K.mark("w", "ldg")
    K.mark("cs", "ld")
    K.do("vector", lambda e: e.memset(t["nrowS"][0:1, :], 0.0))
    K.wait("tensor", "ldg", "w")
    K.wait("tensor", "ldk0", "r1l")
    rcs = list(range(8, 24)) + list(range(0, 8))
    pieces = [(rc, bl) for rc in rcs for bl in range(4)]
    kind_of = lambda i: ("q" if pieces[i][0] < 8 else
                         ("k" if pieces[i][0] < 16 else "v"))
    nn_of = {}
    nn = 0
    for i in range(96):
        nn_of[i] = nn
        nn += 1
    nn_of = {}
    nn = 0
    for i in range(96):
        if kind_of(i) != "v":
            nn_of[i] = nn
            nn += 1

    def emit_rot(i):
        if i < 0 or kind_of(i) == "v":
            return
        bslot = i % 2
        K.wait("tensor", "ac", ("qkev", i))
        prev = i - 2
        if prev >= 0 and kind_of(prev) != "v":
            K.wait("tensor", "ve", ("roped", prev))
        K.do("tensor", lambda e, bslot=bslot: e.matmul(
            t[f"rotps{bslot}"][:, :], t["rotm"][:, :], t[f"pb{bslot}"][:, :],
            start=True, stop=True), inc="te")
        K.mark(("rot", i), "te")

    def emit_nmm(i):
        if i < 0 or i not in nn_of:
            return
        n_ = nn_of[i]
        K.wait("tensor", "ve", ("sqp", i))
        if n_ >= 1:
            K.wait("tensor", "ve", ("nred", n_ - 1))
        K.do("tensor", lambda e: e.matmul(
            t["pn0"][0:1, :], t["ones_col"][:, :], t["sqp"][:, :],
            start=True, stop=True), inc="te")
        K.mark(("nmm", n_), "te")

    def nred_col(i):
        rc, bl = pieces[i]
        if rc < 8:
            return (rc % 8) * 4 + bl
        return 32 + (rc % 8) * 4 + bl

    for i in range(96):
        rc, bl = pieces[i]
        kind = kind_of(i)
        psq = t[f"psA{i % 2}"]
        bslot = i % 2
        if i >= 2:
            K.wait("tensor", "ac", ("qkev", i - 2))
        for kc in range(8):
            K.do("tensor", lambda e, rc=rc, kc=kc, bl=bl, psq=psq: e.matmul(
                psq[:, :],
                t["qkvw"][:, kc * 3072 + rc * 128: kc * 3072 + (rc + 1) * 128],
                t[f"r1Tb{bl}"][:, kc * 512:(kc + 1) * 512],
                start=(kc == 0), stop=(kc == 7)), inc="te" if kc == 7 else None)
        K.mark(("qkmm", i), "te")
        emit_rot(i - 1)
        emit_nmm(i - 2)
        K.wait("scalar", "te", ("qkmm", i))
        if i >= 2:
            if kind_of(i - 2) == "v":
                K.wait("scalar", f"sp{(i - 2) % 4}", ("qksp", i - 2))
            else:
                K.wait("scalar", "ve", ("roped", i - 2))
        if kind == "q":
            K.do("scalar", lambda e, rc=rc, psq=psq, bslot=bslot: e.activation(
                t[f"pb{bslot}"][:, :], psq[:, :], AF.Identity,
                bias=t["qkvb_qs"][:, rc:rc + 1], scale=0.125), inc="ac")
        else:
            K.do("scalar", lambda e, rc=rc, psq=psq, bslot=bslot: e.activation(
                t[f"pb{bslot}"][:, :], psq[:, :], AF.Identity,
                bias=t["qkvb_c"][:, rc:rc + 1], scale=1.0), inc="ac")
        K.mark(("qkev", i), "ac")
        spslot = i % 4
        spsem = f"sp{spslot}"
        if kind == "v":
            K.wait("sync", "ac", ("qkev", i))
            if i >= 4:
                K.wait("sync", spsem, ("qksp", i - 4))
            K.do("sync", lambda e, rc=rc, bl=bl, bslot=bslot: e.dma_start(
                ap["cc2_in"][8 + rc - 16][:, bl * 512:(bl + 1) * 512],
                t[f"pb{bslot}"][:, :]), inc=spsem, inc_n=16)
            K.mark(("qksp", i), spsem)
        else:
            K.wait("vector", "ac", ("qkev", i))
            if i == 0:
                K.wait("vector", "ld", "cs")
            K.do("vector", lambda e, bl=bl, bslot=bslot: e.tensor_tensor(
                t["ropet1"][:, :], t[f"pb{bslot}"][:, :],
                t["cosb"][:, bl * 512:(bl + 1) * 512], op=OP.mult))
            K.wait("vector", "te", ("rot", i))
            K.do("vector", lambda e, bl=bl, bslot=bslot: e.tensor_tensor(
                t["ropet2"][:, :], t[f"rotps{bslot}"][:, :],
                t["sinb"][:, bl * 512:(bl + 1) * 512], op=OP.mult))
            K.dr("vector")
            if kind == "q":
                K.do("vector", lambda e, rc=rc, bl=bl: e.tensor_add(
                    t["qTf"][:, rc * TO + bl * 512: rc * TO + bl * 512 + 512],
                    t["ropet1"][:, :], t["ropet2"][:, :]), inc="ve")
                K.mark(("roped", i), "ve")
            else:
                if i >= 4:
                    K.wait("vector", spsem, ("qksp", i - 4))
                K.do("vector", lambda e, spslot=spslot: e.tensor_add(
                    t[f"ro{spslot}"][:, :], t["ropet1"][:, :], t["ropet2"][:, :]),
                    inc="ve")
                K.mark(("roped", i), "ve")
                K.wait("sync", "ve", ("roped", i))
                K.do("sync", lambda e, rc=rc, bl=bl, spslot=spslot: e.dma_start(
                    ap["cc2_in"][rc - 8][:, bl * 512:(bl + 1) * 512],
                    t[f"ro{spslot}"][:, :]), inc=spsem, inc_n=16)
                K.mark(("qksp", i), spsem)
            n_ = nn_of[i]
            if n_ >= 1:
                K.wait("vector", "te", ("nmm", n_ - 1))
            if kind == "q":
                K.do("vector", lambda e, rc=rc, bl=bl: e.tensor_tensor(
                    t["sqp"][:, :],
                    t["qTf"][:, rc * TO + bl * 512: rc * TO + bl * 512 + 512],
                    t["qTf"][:, rc * TO + bl * 512: rc * TO + bl * 512 + 512],
                    op=OP.mult), inc="ve")
            else:
                K.do("vector", lambda e, spslot=spslot: e.tensor_tensor(
                    t["sqp"][:, :], t[f"ro{spslot}"][:, :], t[f"ro{spslot}"][:, :],
                    op=OP.mult), inc="ve")
            K.mark(("sqp", i), "ve")
            if n_ >= 1:
                K.wait("vector", "te", ("nmm", n_ - 1))
                ppn = [p for p, v in nn_of.items() if v == n_ - 1][0]
                K.do("vector", lambda e, c=nred_col(ppn): e.reduce_max(
                    t["nrowF"][0:1, c:c + 1], t["pn0"][0:1, :],
                    axis=AX.X), inc="ve")
                K.mark(("nred", n_ - 1), "ve")
        if i == 63:
            for sl in range(4):
                last = max(p for p in range(64) if p % 4 == sl)
                K.wait("gpsimd", f"sp{sl}", ("qksp", last))
            K.do("gpsimd", lambda e: e.collective_compute(
                "AllGather", OP.bypass,
                replica_groups=[[0, 1], [2, 3], [4, 5], [6, 7]],
                ins=[K.h["cc2_in"].ap().opt()], outs=[K.h["cc2_out"].ap().opt()]),
                inc="gp")
            K.mark("cc2", "gp")
    emit_rot(95)
    emit_nmm(94)
    emit_nmm(95)
    K.wait("vector", "te", ("nmm", nn - 1))
    ppn = [p for p, v in nn_of.items() if v == nn - 1][0]
    K.do("vector", lambda e, c=nred_col(ppn): e.reduce_max(
        t["nrowF"][0:1, c:c + 1], t["pn0"][0:1, :], axis=AX.X), inc="ve")
    K.mark(("nred", nn - 1), "ve")
    K.dr("vector")
    K.do("vector", lambda e: e.tensor_copy(t["nrowS"][0:1, 0:64], t["nrowF"][0:1, :]),
         inc="ve")
    K.mark("nrows", "ve")
    K.wait("sync", "ve", "nrows")
    K.do("sync", lambda e: e.dma_start(ap["ccn_in"], t["nrowS"][0:1, :]),
         inc="sp", inc_n=16)
    K.mark("nsp", "sp")
    K.wait("gpsimd", "sp", "nsp")
    K.do("gpsimd", lambda e: e.collective_compute(
        "AllReduce", OP.max, replica_groups=[[0, 1], [2, 3], [4, 5], [6, 7]],
        ins=[K.h["ccn_in"].ap().opt()], outs=[K.h["ccn_out"].ap().opt()]),
        inc="gp")
    K.mark("ccn", "gp")


def p5(K):
    t, ap = K.t, K.ap
    K.waitx("gpsimd", "gp", "p4b", "ccn")
    K.do("gpsimd", lambda e: e.dma_start(t["nrowS"][0:1, :], ap["ccn_out"]),
         inc="ldg", inc_n=16)
    K.mark("nl", "ldg")
    K.wait("vector", "ldg", "nl")
    K.do("vector", lambda e: e.reduce_max(t["nrm"][0:1, 0:1], t["nrowS"][0:1, 0:32], axis=AX.X))
    K.do("vector", lambda e: e.reduce_max(t["nrm"][0:1, 1:2], t["nrowS"][0:1, 32:64], axis=AX.X))
    K.dr("vector")
    K.do("vector", lambda e: e.tensor_tensor(
        t["nrm"][0:1, 2:3], t["nrm"][0:1, 0:1], t["nrm"][0:1, 1:2], op=OP.mult), inc="ve")
    K.mark("nprod", "ve")
    K.wait("scalar", "ve", "nprod")
    K.do("scalar", lambda e: e.activation(
        t["nrm"][0:1, 3:4], t["nrm"][0:1, 2:3], AF.Sqrt,
        bias=t["eps6c"][0:1, 0:1], scale=1.0), inc="ac")
    K.mark("nsqrt", "ac")
    K.wait("vector", "ac", "nsqrt")
    K.do("vector", lambda e: e.tensor_scalar(
        t["nrm"][0:1, 4:5], t["nrm"][0:1, 3:4], -1.0, 30.0, op0=OP.mult, op1=OP.add))
    K.dr("vector")
    K.do("vector", lambda e: e.tensor_copy(t["nrmb"][0:1, 0:1], t["nrm"][0:1, 4:5]), inc="ve")
    K.mark("nbf", "ve")
    K.wait("tensor", "ve", "nbf")
    K.do("tensor", lambda e: e.matmul(
        t["accA2"][:, 0:1], t["ones_r128"][:, :], t["nrmb"][0:1, 0:1],
        start=True, stop=True), inc="te")
    K.mark("nbc", "te")
    K.wait("scalar", "te", "nbc")
    K.do("scalar", lambda e: e.copy(t["mbarF"][:, :], t["accA2"][:, 0:1]), inc="ac")
    K.mark("mbar", "ac")
    for pj in range(2):
        K.do("vector", lambda e, pj=pj: e.memset(t[f"va{pj}"][:, :], 0.0))
        K.do("vector", lambda e, pj=pj: e.memset(
            t[f"va{pj}"][:, :].rearrange("p (kc c) -> p kc c", c=128)[:, :, 64:65], 1.0))
        K.do("vector", lambda e, pj=pj: e.memset(t[f"vb{pj}"][:, :], 0.0))
        K.do("vector", lambda e, pj=pj: e.memset(
            t[f"vb{pj}"][:, :].rearrange("p (kc c) -> p kc c", c=128)[:, :, 0:1], 1.0),
            inc="ve")
    K.mark("vinit", "ve")
    K.wait("tensor", "ve", "vinit")

    def pv4(kc, PT, st_, sp_, pj, aAn):
        K.do("tensor", lambda e, kc=kc, PT=PT, st_=st_, sp_=sp_, pj=pj, aAn=aAn:
             e.matmul(
            t[aAn][:, :], t[f"va{pj}"][:, kc * 128:(kc + 1) * 128],
            PT[:, 0:512], start=st_, stop=sp_))
        K.do("tensor", lambda e, kc=kc, PT=PT, st_=st_, sp_=sp_, pj=pj: e.matmul(
            t["accB"][:, :], t[f"vb{pj}"][:, kc * 128:(kc + 1) * 128],
            PT[:, 512:1024], start=st_, stop=sp_), inc="te")

    def bc_mm(u):
        aAn = "accA" if u % 2 == 0 else "accA2"
        K.wait("tensor", "ve", ("accfree", u))
        K.do("tensor", lambda e, aAn=aAn: e.matmul(
            t[aAn][0:64, :], t["ones_r64"][:, :], t["sums"][0:1, 0:512],
            start=True, stop=True))
        K.do("tensor", lambda e, aAn=aAn: e.matmul(
            t[aAn][64:128, :], t["ones_r64"][:, :], t["sums"][0:1, 512:1024],
            start=True, stop=True), inc="te")
        K.mark(("bc", u), "te")

    def norm_consume(u):
        aAn = "accA" if u % 2 == 0 else "accA2"
        j_, qc_ = divmod(u, 4)
        cols = slice(j_ * TO + qc_ * 512, j_ * TO + qc_ * 512 + 512)
        K.wait("vector", "te", ("bc", u))

        def _lp(e, fn, *a, **kw):
            with K.nc.allow_low_precision(reason="softmax normalization in bf16"):
                return fn(*a, **kw)
        K.do("vector", lambda e, aAn=aAn: _lp(
            e, e.tensor_copy, t["bcbA"][0:64, :], t[aAn][0:64, :]))
        K.do("vector", lambda e, aAn=aAn: _lp(
            e, e.tensor_copy, t["bcbB"][64:128, :], t[aAn][64:128, :]), inc="ve")
        K.mark(("bcc", u), "ve")
        K.dr("vector")
        K.do("vector", lambda e: _lp(
            e, e.reciprocal, t["bcbA"][0:64, :], t["bcbA"][0:64, :]))
        K.do("vector", lambda e: _lp(
            e, e.reciprocal, t["bcbB"][64:128, :], t["bcbB"][64:128, :]))
        K.dr("vector")
        K.do("vector", lambda e, cols=cols: e.tensor_tensor(
            t["yTf"][0:64, cols], t["yTf"][0:64, cols], t["bcbA"][0:64, :],
            op=OP.mult))
        K.do("vector", lambda e, cols=cols: e.tensor_tensor(
            t["yTf"][64:128, cols], t["yTf"][64:128, cols], t["bcbB"][64:128, :],
            op=OP.mult), inc="ve")
        K.mark(("norm", u), "ve")

    for j in range(8):
        pj = j % 2
        if j >= 2:
            K.wait("sync", "te", ("pair_done", j - 2))
        if j == 0:
            K.waitx("sync", "gp", "p4b", "cc2")
        K.do("sync", lambda e, j=j, pj=pj: e.dma_start(
            t[f"kb{pj}"][:, :].rearrange("p (g c) -> p g c", g=2),
            ap["cc2_out"][:, j].rearrange("g p c -> p g c")), inc=f"ldk{pj}", inc_n=16)
        K.do("sync", lambda e, j=j, pj=pj: e.dma_start(
            t[f"vtb{pj}"][:, :].rearrange("p (g c) -> p g c", g=2),
            ap["cc2_out"][:, 8 + j].rearrange("g p c -> p g c")), inc=f"ldw{pj}", inc_n=16)
        K.mark(("kl", j), f"ldk{pj}")
        K.mark(("vl", j), f"ldw{pj}")
        K.wait("tensor", f"ldw{pj}", ("vl", j))
        if j >= 2:
            K.wait("scalar", "te", ("pair_done", j - 2))
        for g in range(4):
            if j * 4 + g >= 1:
                K.wait("tensor", "ac", ("vev", j * 4 + g - 1))
            for i in range(8):
                kc = g * 8 + i
                K.do("tensor", lambda e, kc=kc, i=i, pj=pj: e.transpose(
                    t["ptpv"][:, i * 128:(i + 1) * 128],
                    t[f"vtb{pj}"][:, kc * 128:(kc + 1) * 128], t["ident"][:, :]),
                    inc="te" if i == 7 else None)
            K.mark(("vtp", j * 4 + g), "te")
            K.wait("scalar", "te", ("vtp", j * 4 + g))
            K.do("scalar", lambda e, g=g, pj=pj: e.copy(
                t[f"va{pj}"][:, :].rearrange("p (kc c) -> p kc c", c=128)[:, g * 8:(g + 1) * 8, 0:64],
                t["ptpv"][:, :].rearrange("p (i c) -> p i c", c=128)[:, :, 0:64]))
            K.do("scalar", lambda e, g=g, pj=pj: e.copy(
                t[f"vb{pj}"][:, :].rearrange("p (kc c) -> p kc c", c=128)[:, g * 8:(g + 1) * 8, 64:128],
                t["ptpv"][:, :].rearrange("p (i c) -> p i c", c=128)[:, :, 64:128]),
                inc="ac")
            K.mark(("vev", j * 4 + g), "ac")
        K.mark(("vready", j), "ac")
        K.wait("tensor", f"ldk{pj}", ("kl", j))
        K.wait("tensor", "ac", ("vready", j))
        for qc in range(4):
            u = j * 4 + qc
            aAn = "accA" if u % 2 == 0 else "accA2"
            for kc in range(32):
                sAB = t[f"sAB{kc % 2}"]
                if kc >= 2:
                    K.wait("tensor", "ac", ("exp", u, kc - 2))
                elif u > 0:
                    K.wait("tensor", "ac", ("exp", u - 1, 30 + kc))
                K.do("tensor", lambda e, j=j, qc=qc, kc=kc, sAB=sAB, pj=pj: e.matmul(
                    sAB[:, 0:512], t[f"kb{pj}"][0:64, kc * 128:(kc + 1) * 128],
                    t["qTf"][0:64, j * TO + qc * 512: j * TO + qc * 512 + 512],
                    start=True, stop=True))
                K.do("tensor", lambda e, j=j, qc=qc, kc=kc, sAB=sAB, pj=pj: e.matmul(
                    sAB[:, 512:1024], t[f"kb{pj}"][64:128, kc * 128:(kc + 1) * 128],
                    t["qTf"][64:128, j * TO + qc * 512: j * TO + qc * 512 + 512],
                    start=True, stop=True), inc="te")
                K.mark(("sc", u, kc), "te")
                if kc == 0 and qc > 0:
                    aAp = "accA" if (u - 1) % 2 == 0 else "accA2"
                    K.wait("tensor", "ac", ("exp", u - 1, 31))
                    pv4(31, t["PT1"], False, True, pj, aAp)
                    K.mark(("pv", u - 1, 31), "te")
                if kc == 6 and u > 0:
                    bc_mm(u - 1)
                if kc >= 1:
                    K.wait("tensor", "ac", ("exp", u, kc - 1))
                    if kc == 1 and u > 0:
                        K.wait("tensor", "ve", ("accBfree", u - 1))
                    if kc == 1 and u >= 2:
                        K.wait("tensor", "ve", ("bcc", u - 2))
                    pv4(kc - 1, t[f"PT{(kc - 1) % 2}"], kc - 1 == 0, False, pj, aAn)
                    K.mark(("pv", u, kc - 1), "te")
            if qc == 3:
                K.wait("tensor", "ac", ("exp", u, 31))
                pv4(31, t["PT1"], False, True, pj, aAn)
                K.mark(("pv", u, 31), "te")
            for kc in range(32):
                K.wait("scalar", "te", ("sc", u, kc))
                if u == 0 and kc == 0:
                    K.wait("scalar", "ac", "mbar")
                K.do("scalar", lambda e, kc=kc: e.activation(
                    t[f"PT{kc % 2}"][:, :], t[f"sAB{kc % 2}"][:, :], AF.Exp,
                    bias=t["mbarF"][:, 0:1], scale=1.0), inc="ac")
                K.mark(("exp", u, kc), "ac")
            if u > 0:
                norm_consume(u - 1)
            K.wait("vector", "te", ("pv", u, 31))
            if u > 0:
                K.wait("vector", "te", ("bc", u - 1))

            def _cp(e, dst, src_):
                with K.nc.allow_low_precision(reason="unnormalized evac in bf16"):
                    return e.tensor_copy(dst, src_)
            K.do("vector", lambda e, aAn=aAn: _cp(
                e, t["sums"][0:1, 0:512], t[aAn][64:65, :]))
            K.do("vector", lambda e: _cp(
                e, t["sums"][0:1, 512:1024], t["accB"][0:1, :]))
            K.do("vector", lambda e, j=j, qc=qc: _cp(
                e, t["yTf"][64:128, j * TO + qc * 512: j * TO + qc * 512 + 512],
                t["accB"][64:128, :]), inc="ve")
            K.mark(("accBfree", u), "ve")
            K.do("vector", lambda e, j=j, qc=qc, aAn=aAn: _cp(
                e, t["yTf"][0:64, j * TO + qc * 512: j * TO + qc * 512 + 512],
                t[aAn][0:64, :]), inc="ve")
            K.mark(("accfree", u), "ve")
        K.mark(("pair_done", j), "te")
    bc_mm(31)
    norm_consume(31)



def p6(K):
    t, ap = K.t, K.ap
    K.do("gpsimd", lambda e: e.dma_start(
        t["outw"][:, :].rearrange("p (kc o) -> p kc o", kc=8),
        ap["outw_t"].rearrange("(kc p) o -> p kc o", p=128)), inc="ldg", inc_n=16)
    K.do("gpsimd", lambda e: e.dma_start(
        t["gateb"][:, :].rearrange("p (kc o) -> p kc o", kc=8),
        ap["gate_t"].rearrange("(kc p) o -> p kc o", p=128)), inc="ldg", inc_n=16)
    K.mark("w", "ldg")
    K.wait("tensor", "ldg", "w")
    for ob in range(4):
        for oc in range(8):
            idx = ob * 8 + oc
            psq = t[f"psA{oc % 2}"]
            if idx >= 2:
                K.wait("tensor", "ac", ("oev", idx - 2))
            for kc in range(8):
                K.do("tensor", lambda e, oc=oc, kc=kc, ob=ob, psq=psq: e.matmul(
                    psq[:, :],
                    t["outw"][:, kc * D + oc * 128: kc * D + (oc + 1) * 128],
                    t["yTf"][:, kc * TO + ob * 512: kc * TO + ob * 512 + 512],
                    start=(kc == 0), stop=(kc == 7)), inc="te" if kc == 7 else None)
            K.mark(("omm", idx), "te")
            K.wait("scalar", "te", ("omm", idx))
            K.do("scalar", lambda e, oc=oc, psq=psq: e.activation(
                t["oT"][:, oc * 512:(oc + 1) * 512], psq[:, :],
                AF.Identity, bias=t["outb_c"][:, oc:oc + 1], scale=1.0), inc="ac")
            K.mark(("oev", idx), "ac")
        for tt in range(4):
            it = ob * 4 + tt
            pt = t[f"ptp{tt % 2}"]
            if it % 4 == 0:
                ic = it // 4
                if ic >= 2:
                    K.wait("sync", "ve", ("zdone", ic * 4 - 5))
                K.do("sync", lambda e, ic=ic: e.dma_start(
                    t[f"x1b{ic % 2}"][:, :].rearrange("p (tt d) -> p tt d", tt=4),
                    ap["x1_sp"][ic * 512:(ic + 1) * 512, :].rearrange(
                        "(tt p) d -> p tt d", p=128)), inc=f"ldx{ic % 2}", inc_n=16)
                K.mark(("x1l", ic), f"ldx{ic % 2}")
            K.wait("tensor", "ac", ("oev", ob * 8 + 7))
            if it >= 2:
                K.wait("tensor", "ac", ("mtok_ev", it - 2))
            for oc in range(8):
                K.do("tensor", lambda e, oc=oc, tt=tt, pt=pt: e.transpose(
                    pt[:, oc * 128:(oc + 1) * 128],
                    t["oT"][:, oc * 512 + tt * 128: oc * 512 + (tt + 1) * 128],
                    t["ident"][:, :]), inc="te" if oc == 7 else None)
            K.mark(("otp", it), "te")
            K.wait("scalar", "te", ("otp", it))
            if it >= 1:
                K.wait("scalar", "ve", ("mtok_used", it - 1))
            K.do("scalar", lambda e, pt=pt: e.copy(
                t["mtok0"][:, :], pt[:, :]), inc="ac")
            K.mark(("mtok_ev", it), "ac")
            K.wait("vector", f"ldx{(it // 4) % 2}", ("x1l", it // 4))
            K.wait("vector", "ac", ("mtok_ev", it))
            if it >= 1:
                K.wait("vector", "te", ("xtp", it - 1))
            ln_tile(K, t["z"][:, it * D:(it + 1) * D],
                    t[f"x1b{(it // 4) % 2}"][:, (it % 4) * D:(it % 4 + 1) * D],
                    extra_add=t["mtok0"][:, :])
            K.mark(("mtok_used", it), "ve")
            ln_tile(K, t["xn_tok"][:, :], t["z"][:, it * D:(it + 1) * D],
                    rms=True, weight_rep=t["n2w"][:, :], eps_col="eps6c")
            K.mark(("zdone", it), "ve")
            K.wait("tensor", "ve", ("zdone", it))
            if it >= 1:
                K.wait("tensor", "ac", ("xev", it - 1))
            for fc in range(8):
                K.do("tensor", lambda e, fc=fc: e.transpose(
                    t["ptpr"][:, fc * 128:(fc + 1) * 128],
                    t["xn_tok"][:, fc * 128:(fc + 1) * 128],
                    t["ident"][:, :]), inc="te" if fc == 7 else None)
            K.mark(("xtp", it), "te")
            K.wait("scalar", "te", ("xtp", it))
            K.do("scalar", lambda e, it=it: e.copy(
                t["xnT"][:, :].rearrange("p (fc c) -> p fc c", fc=8)[:, :, it * 128:(it + 1) * 128],
                t["ptpr"][:, :]), inc="ac")
            K.mark(("xev", it), "ac")
            K.wait("tensor", "ac", ("xev", it))
            if it >= 1:
                K.wait("tensor", "ve", ("glcp", it - 1))
            for kc in range(8):
                K.do("tensor", lambda e, kc=kc, it=it: e.matmul(
                    t["gps"][:, 0:8],
                    t["xnT"][:, kc * TO + it * 128: kc * TO + (it + 1) * 128],
                    t["gateb"][:, kc * 8:(kc + 1) * 8],
                    start=(kc == 0), stop=(kc == 7)), inc="te" if kc == 7 else None)
            K.mark(("gmm", it), "te")
            K.wait("vector", "te", ("gmm", it))
            K.do("vector", lambda e, it=it: e.tensor_copy(
                t["gl"][:, it * 8:(it + 1) * 8], t["gps"][:, 0:8]), inc="ve")
            K.mark(("glcp", it), "ve")
    def sl8(nm, it):
        return t[nm][:, it * 8:(it + 1) * 8]

    def sl1(nm, it):
        return t[nm][:, it:it + 1]
    K.dr("vector")
    for it in range(16):
        K.do("vector", lambda e, it=it: e.reduce_max(
            sl1("gnm1", it), sl8("gl", it), axis=AX.X, negate=True),
            inc="ve" if it == 15 else None)
    K.mark("gnm1", "ve")
    for it in range(16):
        K.do("vector", lambda e, it=it: e.reduce_max(
            sl1("gm1", it), sl8("gl", it), axis=AX.X))
    for it in range(16):
        K.do("vector", lambda e, it=it: e.tensor_single_scalar(
            sl8("gmsk", it), sl8("gl", it), sl1("gm1", it), op=OP.is_ge))
    for it in range(16):
        K.do("vector", lambda e, it=it: e.reduce_sum(
            sl1("gcnt", it), sl8("gmsk", it), axis=AX.X))
    for it in range(16):
        K.do("vector", lambda e, it=it: e.scalar_tensor_tensor(
            sl8("gmskd", it), sl8("gmsk", it), -1e30, sl8("gl", it),
            op0=OP.mult, op1=OP.add))
    for it in range(16):
        K.do("vector", lambda e, it=it: e.reduce_max(
            sl1("gm2", it), sl8("gmskd", it), axis=AX.X))
    for it in range(16):
        K.do("vector", lambda e, it=it: e.tensor_single_scalar(
            sl1("gtie", it), sl1("gcnt", it), 1.5, op=OP.is_ge))
    for it in range(16):
        K.do("vector", lambda e, it=it: e.tensor_sub(
            sl1("gdlt", it), sl1("gm1", it), sl1("gm2", it)))
    for it in range(16):
        K.do("vector", lambda e, it=it: e.scalar_tensor_tensor(
            sl1("gthr", it), sl1("gtie", it), sl1("gdlt", it), sl1("gm2", it),
            op0=OP.mult, op1=OP.add), inc="ve" if it == 15 else None)
    K.mark("gthr", "ve")
    K.wait("scalar", "ve", "gnm1")
    for it in range(16):
        K.do("scalar", lambda e, it=it: e.activation(
            sl8("gexp", it), sl8("gl", it), AF.Exp, bias=sl1("gnm1", it),
            scale=1.0), inc="ac" if it == 15 else None)
    K.mark("gexp", "ac")
    K.wait("vector", "ac", "gexp")
    for it in range(16):
        K.do("vector", lambda e, it=it: e.tensor_single_scalar(
            sl8("gmsk2", it), sl8("gl", it), sl1("gthr", it), op=OP.is_ge))
    for it in range(16):
        K.do("vector", lambda e, it=it: e.tensor_tensor(
            sl8("gme", it), sl8("gexp", it), sl8("gmsk2", it), op=OP.mult))
    for it in range(16):
        K.do("vector", lambda e, it=it: e.reduce_sum(
            sl1("gsum", it), sl8("gme", it), axis=AX.X))
    for it in range(16):
        K.do("vector", lambda e, it=it: e.reciprocal(
            sl1("grcp", it), sl1("gsum", it)))
    for it in range(16):
        K.do("vector", lambda e, it=it: e.tensor_scalar_mul(
            sl8("combb", it), sl8("gme", it), sl1("grcp", it)),
            inc="ve" if it == 15 else None)
    K.mark("done_ve", "ve")



def p7(K):
    t, ap = K.t, K.ap
    for e_ in range(NE):
        for k in range(16):
            kk = e_ * 16 + k
            wsem = f"ldw{k % 2}"
            if kk >= 2:
                K.wait("gpsimd", "te", ("p1mm", kk - 2))
            K.do("gpsimd", lambda e, e_=e_, k=k: e.dma_start(
                t[f"w1b{k % 2}"][:, :].rearrange("p (kc c) -> p kc c", kc=8),
                ap["w1_s"][e_, k].rearrange("(kc p) c -> p kc c", p=128)),
                inc=wsem, inc_n=16)
            K.do("gpsimd", lambda e, e_=e_, k=k: e.dma_start(
                t[f"w2b{k % 2}"][:, :].rearrange("p (kc c) -> p kc c", kc=8),
                ap["w2_s"][e_, k].rearrange("(kc p) c -> p kc c", p=128)),
                inc=wsem, inc_n=16)
            K.mark(("wl", kk), wsem)
            K.wait("tensor", wsem, ("wl", kk))
            b1col = t["b1_c"][:, e_ * 16 + k: e_ * 16 + k + 1]
            b2col = t["b2_c"][:, e_ * 16 + k: e_ * 16 + k + 1]
            for ns in range(4):
                ik = kk * 4 + ns
                h1 = t[f"h1_{ns % 2}"]
                h2 = t[f"h2_{ns % 2}"]
                if ik >= 2:
                    K.wait("tensor", "ve", ("gw", ik - 2))
                for kc in range(8):
                    K.do("tensor", lambda e, kc=kc, ns=ns, k=k, h1=h1: e.matmul(
                        h1[:, :], t[f"w1b{k % 2}"][:, kc * 128:(kc + 1) * 128],
                        t["xnT"][:, kc * TO + ns * 512: kc * TO + ns * 512 + 512],
                        start=(kc == 0), stop=(kc == 7)),
                        inc="te" if kc == 7 else None)
                K.mark(("h1m", ik), "te")
                K.wait("scalar", "te", ("h1m", ik))
                if ik >= 2:
                    K.wait("scalar", "ve", ("gw", ik - 2))
                K.do("scalar", lambda e, h1=h1, b1col=b1col, ns=ns: e.activation(
                    t[f"g1_{ns % 2}"][:, :], h1[:, :], AF.Silu, bias=b1col,
                    scale=1.0), inc="ac")
                K.mark(("sil", ik), "ac")
                for kc in range(8):
                    K.do("tensor", lambda e, kc=kc, ns=ns, k=k, h2=h2: e.matmul(
                        h2[:, :], t[f"w2b{k % 2}"][:, kc * 128:(kc + 1) * 128],
                        t["xnT"][:, kc * TO + ns * 512: kc * TO + ns * 512 + 512],
                        start=(kc == 0), stop=(kc == 7)),
                        inc="te" if kc == 7 else None)
                K.mark(("h2m", ik), "te")
                K.mark(("p1mm", kk), "te")
                K.wait("vector", "te", ("h2m", ik))
                K.wait("vector", "ac", ("sil", ik))
                K.do("vector", lambda e, k=k, ns=ns, h2=h2, b2col=b2col: e.scalar_tensor_tensor(
                    t["g"][:, k * TO + ns * 512: k * TO + ns * 512 + 512],
                    h2[:, :], b2col, t[f"g1_{ns % 2}"][:, :],
                    op0=OP.add, op1=OP.mult), inc="ve")
                K.mark(("gw", ik), "ve")
        K.do("gpsimd", lambda e, e_=e_: e.dma_start(
            t["b3r"][:, :], ap["b3_rep"][e_]), inc="ld", inc_n=16)
        K.mark(("b3", e_), "ld")
        for oh in range(2):
            io = e_ * 2 + oh
            wsem3 = f"ldk{oh % 2}"
            if io >= 1:
                K.wait("gpsimd", "te", ("p2mm", io - 1))
            K.do("gpsimd", lambda e, e_=e_, oh=oh: e.dma_start(
                t["w3b"][:, :].rearrange("p (k c) -> p k c", k=16),
                ap["w3_s"][e_][:, :, oh * 512:(oh + 1) * 512].rearrange(
                    "k p c -> p k c")), inc=wsem3, inc_n=16)
            K.mark(("w3l", io), wsem3)
            K.wait("tensor", wsem3, ("w3l", io))
            K.wait("tensor", "ve", ("gw", e_ * 64 + 63))
            for tt in range(16):
                itt = io * 16 + tt
                yps = t[f"yps{tt % 2}"]
                if itt >= 2:
                    K.wait("tensor", "ve", ("zacc", itt - 2))
                for k in range(16):
                    K.do("tensor", lambda e, k=k, tt=tt, yps=yps: e.matmul(
                        yps[:, :], t["g"][:, k * TO + tt * 128:(k * TO + (tt + 1) * 128)],
                        t["w3b"][:, k * 512:(k + 1) * 512],
                        start=(k == 0), stop=(k == 15)),
                        inc="te" if k == 15 else None)
                K.mark(("ymm", itt), "te")
                K.mark(("p2mm", io), "te")
                K.wait("vector", "te", ("ymm", itt))
                if tt == 0 and oh == 0:
                    K.wait("vector", "ld", ("b3", e_))
                comb = t["combb"][:, tt * 8 + e_: tt * 8 + e_ + 1]
                zsl = t["z"][:, tt * D + oh * 512: tt * D + oh * 512 + 512]
                K.do("vector", lambda e, yps=yps, comb=comb, zsl=zsl: e.scalar_tensor_tensor(
                    zsl, yps[:, :], comb, zsl, op0=OP.mult, op1=OP.add))
                K.do("vector", lambda e, oh=oh, comb=comb, zsl=zsl: e.scalar_tensor_tensor(
                    zsl, t["b3r"][:, oh * 512:(oh + 1) * 512], comb, zsl,
                    op0=OP.mult, op1=OP.add), inc="ve")
                K.mark(("zacc", itt), "ve")
    K.mark("done", "ve")



def p8(K):
    t, ap = K.t, K.ap
    for it in range(16):
        if it % 4 == 0:
            ic = it // 4
            if ic >= 2:
                K.wait("sync", "ve", ("odone", ic * 4 - 5))
            K.do("sync", lambda e, ic=ic: e.dma_start(
                t[f"x1b{ic % 2}"][:, :].rearrange("p (tt d) -> p tt d", tt=4),
                ap["x1_sp"][ic * 512:(ic + 1) * 512, :].rearrange(
                    "(tt p) d -> p tt d", p=128)), inc=f"ldx{ic % 2}", inc_n=16)
            K.mark(("x1l", ic), f"ldx{ic % 2}")
        K.wait("vector", f"ldx{(it // 4) % 2}", ("x1l", it // 4))
        if it >= 2:
            K.wait("vector", f"sp{it % 2}", ("ost", it - 2))
        ln_tile(K, t[f"ot{it % 2}"][:, :], t["z"][:, it * D:(it + 1) * D],
                extra_add=t[f"x1b{(it // 4) % 2}"][:, (it % 4) * D:(it % 4 + 1) * D])
        K.mark(("odone", it), "ve")
        K.wait("sync", "ve", ("odone", it))
        K.do("sync", lambda e, it=it: e.dma_start(
            ap["out_own"][it * 128:(it + 1) * 128, :], t[f"ot{it % 2}"][:, :]),
            inc=f"sp{it % 2}", inc_n=16)
        K.mark(("ost", it), f"sp{it % 2}")
    K.wait("sync", "sp0", ("ost", 14))
    K.wait("sync", "sp1", ("ost", 15))



def _shared_arrays(ins):
    f32 = np.float32
    sh = {}
    sh["wq_t"] = np.ascontiguousarray(np.asarray(ins["L0_attn_Wq"], f32).T)
    sh["wkv_t"] = np.ascontiguousarray(np.asarray(ins["L0_attn_Wkv"], f32).T)
    sh["wo_t"] = np.ascontiguousarray(np.asarray(ins["L0_attn_Wo"], f32).T)
    sh["qkvw_t"] = np.ascontiguousarray(np.asarray(ins["L0_ffn_qkv_w"], f32).T)
    sh["outw_t"] = np.ascontiguousarray(np.asarray(ins["L0_ffn_out_w"], f32).T)
    sh["gate_t"] = np.ascontiguousarray(np.asarray(ins["L0_ffn_gate_w"], f32).T)
    w1t = np.ascontiguousarray(np.transpose(np.asarray(ins["L0_ffn_w1"], f32), (0, 2, 1)))
    w2t = np.ascontiguousarray(np.transpose(np.asarray(ins["L0_ffn_w2"], f32), (0, 2, 1)))
    w3t = np.ascontiguousarray(np.transpose(np.asarray(ins["L0_ffn_w3"], f32), (0, 2, 1)))
    sh["w1_s"] = np.ascontiguousarray(w1t.reshape(8, 1024, 16, 128).transpose(0, 2, 1, 3))
    sh["w2_s"] = np.ascontiguousarray(w2t.reshape(8, 1024, 16, 128).transpose(0, 2, 1, 3))
    sh["w3_s"] = np.ascontiguousarray(w3t.reshape(8, 16, 128, 1024))
    sh["bq_c"] = np.ascontiguousarray(np.asarray(ins["L0_attn_bq"], f32).reshape(8, 128).T)
    sh["bkv_c"] = np.ascontiguousarray(np.asarray(ins["L0_attn_bkv"], f32).reshape(1, 128).T)
    sh["bo_c"] = np.ascontiguousarray(np.asarray(ins["L0_attn_bo"], f32).reshape(8, 128).T)
    sh["qkvb_c"] = np.ascontiguousarray(np.asarray(ins["L0_ffn_qkv_b"], f32).reshape(24, 128).T)
    sh["outb_c"] = np.ascontiguousarray(np.asarray(ins["L0_ffn_out_b"], f32).reshape(8, 128).T)
    sh["b1_c"] = np.ascontiguousarray(
        np.asarray(ins["L0_ffn_b1"], f32).reshape(8, 16, 128).transpose(2, 0, 1).reshape(128, 128))
    sh["b2_c"] = np.ascontiguousarray(
        np.asarray(ins["L0_ffn_b2"], f32).reshape(8, 16, 128).transpose(2, 0, 1).reshape(128, 128))
    sh["b3_rep"] = np.ascontiguousarray(
        np.broadcast_to(np.asarray(ins["L0_ffn_b3"], f32)[:, None, :], (8, 128, 1024)))
    sh["n1w_rep"] = np.ascontiguousarray(
        np.broadcast_to(np.asarray(ins["L0_ffn_norm1_w"], f32)[None, :], (128, 1024)))
    sh["n2w_rep"] = np.ascontiguousarray(
        np.broadcast_to(np.asarray(ins["L0_ffn_norm2_w"], f32)[None, :], (128, 1024)))
    sh["ident_bf"] = np.eye(128, dtype=ml_dtypes.bfloat16)
    sh["ones_r64"] = np.ones((1, 64), dtype=ml_dtypes.bfloat16)
    sh["ones_r128"] = np.ones((1, 128), dtype=ml_dtypes.bfloat16)
    sh["ones_col"] = np.ones((128, 1), dtype=ml_dtypes.bfloat16)
    oh = np.zeros((1, 128), dtype=ml_dtypes.bfloat16); oh[0, 64:] = 1
    sh["ones_hi128"] = oh
    rm = np.zeros((128, 128), dtype=ml_dtypes.bfloat16)
    for blk in range(2):
        for dd in range(64):
            rm[blk * 64 + (dd + 32) % 64, blk * 64 + dd] = 1
    sh["rotm"] = rm
    inv_freq = 1.0 / 10000.0 ** (np.arange(0, 64, 2, dtype=np.float64) / 64)
    tpos = np.arange(T, dtype=np.float64)
    emb = np.concatenate([tpos[:, None] * inv_freq, tpos[:, None] * inv_freq], axis=1)
    cos = np.cos(emb).astype(f32).T
    sin = np.sin(emb).astype(f32).T
    sin_eff = sin.copy()
    sin_eff[:32] = -sin[:32]
    sh["cosT"] = np.ascontiguousarray(np.concatenate([cos, cos], axis=0))
    sh["sinT"] = np.ascontiguousarray(np.concatenate([sin_eff, sin_eff], axis=0))
    return sh



def build_program():
    import contextlib
    nc = bass.Bass("TRN2", target_bir_lowering=False,
                   dynamic_dma_scratch_size=16384)
    K = Ctx(nc)
    ap = K.ap

    def din(name, shape, dt=F32):
        ap[name] = nc.dram_tensor(name, shape, dt, kind="ExternalInput").ap()

    din("x_own", [TO, D])
    din("x_q", [TO, D])
    din("cosO", [128, TO]); din("sinO", [128, TO])
    din("wq_t", [D, D]); din("wkv_t", [D, 128]); din("wo_t", [D, D])
    din("qkvw_t", [D, 3 * D]); din("outw_t", [D, D]); din("gate_t", [D, 8])
    din("w1_s", [8, 16, 1024, 128]); din("w2_s", [8, 16, 1024, 128])
    din("w3_s", [8, 16, 128, 1024])
    din("bq_c", [128, 8]); din("bkv_c", [128, 1]); din("bo_c", [128, 8])
    din("qkvb_c", [128, 24]); din("outb_c", [128, 8])
    din("b1_c", [128, 128]); din("b2_c", [128, 128]); din("b3_rep", [8, 128, 1024])
    din("n1w_rep", [128, D]); din("n2w_rep", [128, D])
    din("ident_bf", [128, 128], BF16)
    din("ones_r64", [1, 64], BF16); din("ones_r128", [1, 128], BF16)
    din("ones_hi128", [1, 128], BF16)
    din("rotm", [128, 128], BF16)
    din("ones_col", [128, 1], BF16)
    ap["out_own"] = nc.dram_tensor("out_own", [TO, D], F32, kind="ExternalOutput").ap()
    ap["x1_sp"] = nc.dram_tensor("x1_sp", [TO, D], F32, kind="Internal").ap()
    ap["r1_sp"] = nc.dram_tensor("r1_sp", [8, 128, TO], BF16, kind="Internal").ap()
    K.h = {}
    for nm, shp, dt_ in (("cc1_in", [128, TO], BF16), ("cc1_out", [2, 128, TO], BF16),
                         ("cc2_in", [16, 128, TO], BF16),
                         ("cc2_out", [2, 16, 128, TO], BF16),
                         ("ccn_in", [1, 128], F32), ("ccn_out", [1, 128], F32)):
        K.h[nm] = nc.dram_tensor(nm, shp, dt_, kind="Internal")
        ap[nm] = K.h[nm].ap()
    if DEBUG:
        ap["dbg_qTa"] = nc.dram_tensor("dbg_qTa", [128, 8 * TO], BF16,
                                       kind="ExternalOutput").ap()
        ap["dbg_kvT"] = nc.dram_tensor("dbg_kvT", [128, T], BF16,
                                       kind="ExternalOutput").ap()
        ap["dbg_yTa"] = nc.dram_tensor("dbg_yTa", [128, 8 * TO], BF16,
                                       kind="ExternalOutput").ap()
        ap["dbg_x1"] = nc.dram_tensor("dbg_x1", [TO, D], F32, kind="ExternalOutput").ap()
        ap["dbg_r1"] = nc.dram_tensor("dbg_r1", [8, 128, TO], BF16, kind="ExternalOutput").ap()


    stack = contextlib.ExitStack()
    t = K.t

    def sb(name, shape, dt=F32, **kw):
        t[name] = stack.enter_context(nc.sbuf_tensor(uname("s_" + name), shape, dt, **kw))

    def psum(name, shape, dt=F32):
        t[name] = stack.enter_context(nc.psum_tensor(uname("ps_" + name), shape, dt))

    with stack:
        for s in SEMS:
            K.sems[s] = stack.enter_context(nc.semaphore(s))
            K.counts[s] = 0

        sb("bq_c", [128, 8]); sb("bkv_c", [128, 1]); sb("bo_c", [128, 8])
        sb("qkvb_c", [128, 24]); sb("outb_c", [128, 8])
        sb("b1_c", [128, 128]); sb("b2_c", [128, 128])
        sb("n1w", [128, D]); sb("n2w", [128, D])
        sb("bq_s", [128, 8]); sb("qkvb_qs", [128, 8])
        sb("ident", [128, 128], BF16)
        sb("ones_r64", [1, 64], BF16); sb("ones_r128", [1, 128], BF16)
        sb("ones_hi128", [1, 128], BF16)
        sb("ones_col", [128, 1], BF16)
        sb("epsc", [128, 1]); sb("eps6c", [128, 1])
        sb("mbarA", [128, 1]); sb("mbarF", [128, 1])
        sb("tmpf", [128, D]); sb("stats", [128, 8])
        sb("nrm", [1, 8]); sb("nrmb", [1, 1], BF16)

        syta = contextlib.ExitStack()
        t["yTa"] = syta.enter_context(
            nc.sbuf_tensor(uname("s_yTa"), [128, 8 * TO], BF16, side="right"))
        sqkv = contextlib.ExitStack()
        for _nm, _shp in (("qTa", [128, 8 * TO]), ("kvT", [128, T])):
            t[_nm] = sqkv.enter_context(nc.sbuf_tensor(uname("s_" + _nm), _shp, BF16))

        with contextlib.ExitStack() as s1:
            def sb1(name, shape, dt=F32, **kw):
                t[name] = s1.enter_context(nc.sbuf_tensor(uname("s_" + name), shape, dt, **kw))
            def ps1(name, shape, dt=F32):
                t[name] = s1.enter_context(nc.psum_tensor(uname("ps_" + name), shape, dt))
            sb1("wq", [128, 8 * D], BF16)
            sb1("wkv", [128, D], BF16)
            sb1("kvTh", [128, TO], BF16)
            sb1("xb0", [128, 4 * D])
            sb1("ub0", [128, 4 * D], BF16)
            sb1("uT0", [128, 4 * D], BF16)
            ps1("ptp0", [128, 1024], BF16); ps1("ptp1", [128, 1024], BF16)
            ps1("psq0", [128, 512]); ps1("psq1", [128, 512]); ps1("pskv", [128, 512])
            run_phase(K, p0)
            if BUILD_PHASES >= 1:
                run_phase(K, p1)

        if BUILD_PHASES >= 2:
            with contextlib.ExitStack() as s2:
                def sb2(name, shape, dt=F32, **kw):
                    t[name] = s2.enter_context(nc.sbuf_tensor(uname("s_" + name), shape, dt, **kw))
                sb2("kdup", [128, T], BF16)
                sb2("v_lo", [128, 32 * 128], BF16)
                sb2("v_hi", [128, 32 * 128], BF16)
                sb2("sqb", [128, T], BF16)
                sb2("PT0", [128, 1024], BF16); sb2("PT1", [128, 1024], BF16)
                sb2("sums", [1, 1024], BF16)
                sb2("bcbA", [128, 512], BF16); sb2("bcbB", [128, 512], BF16)
                sb2("nrowq", [1, 64]); sb2("nrowk", [1, 8])
                with contextlib.ExitStack() as sp2:
                    for nm, shp, dt_ in (("ptpv", [128, 1024], BF16),
                                         ("pn0", [128, 512], F32),
                                         ("pn1", [128, 512], F32)):
                        t[nm] = sp2.enter_context(nc.psum_tensor(uname("ps_" + nm), shp, dt_))
                    run_phase(K, p2)
                if BUILD_PHASES >= 3:
                    with contextlib.ExitStack() as sp3:
                        for nm in ("sAB0", "sAB1"):
                            t[nm] = sp3.enter_context(
                                nc.psum_tensor(uname("ps_" + nm), [128, 1024], F32))
                        for nm in ("accA", "accB", "accA2", "accB2"):
                            t[nm] = sp3.enter_context(
                                nc.psum_tensor(uname("ps_" + nm), [128, 512], F32))
                        run_phase(K, p3)

        sqkv.close()
        if BUILD_PHASES >= 4:
            with contextlib.ExitStack() as s4:
                def sb4(name, shape, dt=F32, **kw):
                    t[name] = s4.enter_context(nc.sbuf_tensor(uname("s_" + name), shape, dt, **kw))
                def ps4(name, shape, dt=F32):
                    t[name] = s4.enter_context(nc.psum_tensor(uname("ps_" + name), shape, dt))
                sb4("wo", [128, 8 * D], BF16)
                sb4("aT", [128, 8 * 512], BF16)
                sb4("atok0", [128, D], BF16); sb4("atok1", [128, D], BF16)
                sb4("xt0", [128, D]); sb4("xt1", [128, D])
                sb4("x1t0", [128, D]); sb4("x1t1", [128, D])
                sb4("v0t", [128, D])
                sb4("r1tok", [128, D], BF16)
                sb4("r1T", [128, 4 * D], BF16)
                ps4("psA0", [128, 512]); ps4("psA1", [128, 512])
                ps4("ptp0", [128, 1024], BF16); ps4("ptp1", [128, 1024], BF16)
                ps4("ptpr", [128, 1024], BF16)
                run_phase(K, p4a)
            syta.close()
        else:
            syta.close()
        sqtf = contextlib.ExitStack()
        if BUILD_PHASES >= 5:
            t["qTf"] = sqtf.enter_context(
                nc.sbuf_tensor(uname("s_qTf"), [128, 8 * TO], BF16, side="right"))
            with contextlib.ExitStack() as s5:
                def sb5(name, shape, dt=F32, **kw):
                    t[name] = s5.enter_context(nc.sbuf_tensor(uname("s_" + name), shape, dt, **kw))
                def ps5(name, shape, dt=F32):
                    t[name] = s5.enter_context(nc.psum_tensor(uname("ps_" + name), shape, dt))
                sb5("qkvw", [128, 24 * D], BF16)
                sb5("rotm", [128, 128], BF16)
                sb5("cosb", [128, TO]); sb5("sinb", [128, TO])
                for _b in range(4):
                    sb5(f"r1Tb{_b}", [128, 4 * D], BF16)
                sb5("pb0", [128, 512], BF16); sb5("pb1", [128, 512], BF16)
                sb5("ropet1", [128, 512]); sb5("ropet2", [128, 512])
                sb5("ro0", [128, 512], BF16); sb5("ro1", [128, 512], BF16)
                sb5("ro2", [128, 512], BF16); sb5("ro3", [128, 512], BF16)
                sb5("sqp", [128, 512], BF16)
                sb5("nrowF", [1, 64]); sb5("nrowS", [1, 128])
                ps5("psA0", [128, 512]); ps5("psA1", [128, 512])
                ps5("rotps0", [128, 512]); ps5("rotps1", [128, 512])
                ps5("pn0", [128, 512])
                run_phase(K, p4b)

        if BUILD_PHASES >= 6:
            szx = contextlib.ExitStack()
            with contextlib.ExitStack() as s5y:
                t["yTf"] = s5y.enter_context(
                    nc.sbuf_tensor(uname("s_yTf"), [128, 8 * TO], BF16))
                with contextlib.ExitStack() as s5a:
                    def sb5a(name, shape, dt=F32, **kw):
                        t[name] = s5a.enter_context(
                            nc.sbuf_tensor(uname("s_" + name), shape, dt, **kw))
                    def ps5a(name, shape, dt=F32):
                        t[name] = s5a.enter_context(
                            nc.psum_tensor(uname("ps_" + name), shape, dt))
                    sb5a("nrowS", [1, 128])
                    sb5a("kb0", [128, T], BF16); sb5a("kb1", [128, T], BF16)
                    sb5a("vtb0", [128, T], BF16); sb5a("vtb1", [128, T], BF16)
                    sb5a("va0", [128, T], BF16); sb5a("va1", [128, T], BF16)
                    sb5a("vb0", [128, T], BF16); sb5a("vb1", [128, T], BF16)
                    sb5a("PT0", [128, 1024], BF16); sb5a("PT1", [128, 1024], BF16)
                    sb5a("sums", [1, 1024], BF16)
                    sb5a("bcbA", [128, 512], BF16); sb5a("bcbB", [128, 512], BF16)
                    for nm in ("sAB0", "sAB1"):
                        t[nm] = s5a.enter_context(
                            nc.psum_tensor(uname("ps_" + nm), [128, 1024], F32))
                    for nm in ("accA", "accB", "accA2"):
                        ps5a(nm, [128, 512])
                    ps5a("ptpv", [128, 1024], BF16)
                    run_phase(K, p5)
                if DEBUG and BUILD_PHASES == 6:
                    ap["dbg_yTf"] = nc.dram_tensor("dbg_yTf", [128, 8 * TO], BF16,
                                                   kind="ExternalOutput").ap()
                    def pdump_yTf(K):
                        K.do("sync", lambda e: e.dma_start(ap["dbg_yTf"], t["yTf"][:, :]),
                             inc="st", inc_n=16)
                        K.mark("st", "st")
                        K.wait("sync", "st", "st")
                    run_phase(K, pdump_yTf)
                sqtf.close()
                t["z"] = szx.enter_context(
                    nc.sbuf_tensor(uname("s_z"), [128, 16 * D], F32, side="right"))
                t["xnT"] = szx.enter_context(
                    nc.sbuf_tensor(uname("s_xnT"), [128, 8 * TO], BF16, side="right"))
                t["combb"] = szx.enter_context(
                    nc.sbuf_tensor(uname("s_combb"), [128, 128], F32, side="right"))
                with contextlib.ExitStack() as s6:
                    def sb6(name, shape, dt=F32, **kw):
                        t[name] = s6.enter_context(
                            nc.sbuf_tensor(uname("s_" + name), shape, dt, **kw))
                    def ps6(name, shape, dt=F32):
                        t[name] = s6.enter_context(
                            nc.psum_tensor(uname("ps_" + name), shape, dt))
                    sb6("outw", [128, 8 * D], BF16)
                    sb6("gateb", [128, 64], BF16)
                    sb6("oT", [128, 8 * 512], BF16)
                    sb6("mtok0", [128, D], BF16)
                    sb6("x1b0", [128, 4 * D]); sb6("x1b1", [128, 4 * D])
                    sb6("xn_tok", [128, D], BF16)
                    for gnm in ("gl", "gmsk", "gmskd", "gexp", "gmsk2", "gme"):
                        sb6(gnm, [128, 128])
                    for gnm in ("gnm1", "gm1", "gcnt", "gm2", "gtie", "gdlt",
                                "gthr", "gsum", "grcp"):
                        sb6(gnm, [128, 16])
                    ps6("psA0", [128, 512]); ps6("psA1", [128, 512])
                    ps6("ptp0", [128, 1024], BF16); ps6("ptp1", [128, 1024], BF16)
                    ps6("ptpr", [128, 1024], BF16)
                    ps6("gps", [128, 512])
                    if BUILD_PHASES >= 7:
                        run_phase(K, p6)

        if BUILD_PHASES >= 8:
            with contextlib.ExitStack() as s7:
                def sb7(name, shape, dt=F32, **kw):
                    t[name] = s7.enter_context(
                        nc.sbuf_tensor(uname("s_" + name), shape, dt, **kw))
                def ps7(name, shape, dt=F32):
                    t[name] = s7.enter_context(
                        nc.psum_tensor(uname("ps_" + name), shape, dt))
                sb7("g", [128, 16 * TO], BF16)
                sb7("w1b0", [128, D], BF16); sb7("w1b1", [128, D], BF16)
                sb7("w2b0", [128, D], BF16); sb7("w2b1", [128, D], BF16)
                sb7("w3b", [128, 16 * 512], BF16)
                sb7("g1_0", [128, 512], BF16); sb7("g1_1", [128, 512], BF16)
                sb7("b3r", [128, D], BF16)
                ps7("h1_0", [128, 512]); ps7("h1_1", [128, 512])
                ps7("h2_0", [128, 512]); ps7("h2_1", [128, 512])
                ps7("yps0", [128, 512]); ps7("yps1", [128, 512])
                run_phase(K, p7)

        if BUILD_PHASES >= 9:
            with contextlib.ExitStack() as s8:
                def sb8(name, shape, dt=F32, **kw):
                    t[name] = s8.enter_context(
                        nc.sbuf_tensor(uname("s_" + name), shape, dt, **kw))
                sb8("x1b0", [128, 4 * D]); sb8("x1b1", [128, 4 * D])
                sb8("ot0", [128, D]); sb8("ot1", [128, D])
                run_phase(K, p8)
        if BUILD_PHASES >= 6:
            szx.close()

        if DEBUG:
            ap["dbg_ccn"] = nc.dram_tensor("dbg_ccn", [1, 128], F32,
                                           kind="ExternalOutput").ap()
            ap["dbg_mbar"] = nc.dram_tensor("dbg_mbar", [128, 2], F32,
                                            kind="ExternalOutput").ap()
            if "dbg_cc2" not in ap and BUILD_PHASES == 5:
                ap["dbg_cc2"] = nc.dram_tensor("dbg_cc2", [2, 16, 128, TO], BF16,
                                               kind="ExternalOutput").ap()
                ap["dbg_qTf"] = nc.dram_tensor("dbg_qTf", [128, 8 * TO], BF16,
                                               kind="ExternalOutput").ap()
            def pdbg(K):
                if BUILD_PHASES <= 3:
                    K.do("sync", lambda e: e.dma_start(ap["dbg_qTa"], t["qTa"][:, :]),
                         inc="st", inc_n=16)
                    K.do("sync", lambda e: e.dma_start(ap["dbg_kvT"], t["kvT"][:, :]),
                         inc="st", inc_n=16)
                if BUILD_PHASES == 3:
                    K.do("sync", lambda e: e.dma_start(ap["dbg_yTa"], t["yTa"][:, :]),
                         inc="st", inc_n=16)
                if BUILD_PHASES >= 4:
                    K.do("sync", lambda e: e.dma_start(ap["dbg_x1"], ap["x1_sp"]),
                         inc="st", inc_n=16)
                if BUILD_PHASES == 5:
                    K.waitx("sync", "gp", "p4b", "cc2")
                    K.do("sync", lambda e: e.dma_start(ap["dbg_cc2"], ap["cc2_out"]),
                         inc="st", inc_n=16)
                    K.do("sync", lambda e: e.dma_start(ap["dbg_qTf"], t["qTf"][:, :]),
                         inc="st", inc_n=16)

                if BUILD_PHASES >= 6:
                    K.do("sync", lambda e: e.dma_start(ap["dbg_ccn"], ap["ccn_out"]),
                         inc="st", inc_n=16)
                    K.do("sync", lambda e: e.dma_start(
                        ap["dbg_mbar"][:, 0:1], t["mbarA"][:, :],
                        allow_slow_non_contiguous=True), inc="st", inc_n=16)
                    K.do("sync", lambda e: e.dma_start(
                        ap["dbg_mbar"][:, 1:2], t["mbarF"][:, :],
                        allow_slow_non_contiguous=True), inc="st", inc_n=16)
                K.mark("st", "st")
                K.wait("sync", "st", "st")
            run_phase(K, pdbg)

    return nc


def kernel(**ins):
    x = np.ascontiguousarray(np.asarray(ins["L0_attn_x"], dtype=np.float32))
    sh = _shared_arrays(ins)
    qidx = {}
    for hf in range(2):
        idx = []
        for par in range(2):
            for b2 in range(8):
                hc = 2 * b2 + par
                st = hc * 256 + hf * 128
                idx.extend(range(st, st + 128))
        qidx[hf] = np.asarray(idx)
    in_maps = []
    for c in range(8):
        b, hf = c // 2, c % 2
        m = dict(sh)
        m["x_own"] = np.ascontiguousarray(x[b, hf * TO:(hf + 1) * TO])
        m["x_q"] = np.ascontiguousarray(x[b][qidx[hf]])
        m["cosO"] = np.ascontiguousarray(sh["cosT"][:, hf * TO:(hf + 1) * TO])
        m["sinO"] = np.ascontiguousarray(sh["sinT"][:, hf * TO:(hf + 1) * TO])
        m.pop("cosT"); m.pop("sinT")
        in_maps.append(m)
    nc = build_program()
    res = run_bass_kernel_spmd(nc, in_maps, core_ids=list(range(8)))
    out = np.empty((4, T, D), np.float32)
    for c in range(8):
        b, hf = c // 2, c % 2
        out[b, hf * TO:(hf + 1) * TO] = np.asarray(res.results[c]["out_own"])
    kernel.last_results = res
    return out
